# version 01 (final)
# baseline (speedup 1.0000x reference)
import contextlib
import numpy as np

import concourse.bass as bass
import concourse.mybir as mybir
from concourse.bass_utils import run_bass_kernel_spmd

F32 = mybir.dt.float32
BF16 = mybir.dt.bfloat16
ALU = mybir.AluOpType
AF = mybir.ActivationFunctionType
AX = mybir.AxisListType

B, T, C = 2, 8192, 1024
NCORE = 8
N = 2048
G = 8
HW = 2052
C0 = 2
NCOL = 2050
HID = 4096
E_MOE, F_EXP = 16, 512
EPS = 1e-5
C32 = 12582912.0
C64 = 6755399441055744.0

SPANS = [(0, 512), (512, 512), (1024, 512), (1536, 512), (2048, 2)]
TSPANS = [(2, 512), (514, 512), (1026, 512), (1538, 512)]

ENGINES = ("tensor", "vector", "scalar", "gpsimd", "sync")


class Sem:
    def __init__(self, handle, name, step=1):
        self.h, self.name, self.step, self.n = handle, name, step, 0

    def reserve(self, k=1):
        self.n += k
        return self.n


class Plan:
    def __init__(self):
        self.seq = {e: [] for e in ENGINES}
        self.wait_cache = {}

    def emit(self, eng, fn):
        self.seq[eng].append(fn)


def op(plan, eng, fn, inc=None, inc_k=1):
    cnt = inc.reserve(inc_k) if inc is not None else None

    def f(e, fn=fn, inc=inc, inc_k=inc_k):
        inst = fn(e)
        if inc is not None:
            inst.then_inc(inc.h, inc.step * inc_k)

    plan.emit(eng, f)
    return cnt


def wait(plan, eng, sem, cnt):
    if cnt is None or cnt <= 0:
        return
    key = (eng, sem.name)
    if plan.wait_cache.get(key, -1) >= cnt:
        return
    plan.wait_cache[key] = cnt
    plan.emit(eng, lambda e, sem=sem, cnt=cnt: e.wait_ge(sem.h, sem.step * cnt))


def build_program(taps=()):
    nc = bass.Bass("TRN2", target_bir_lowering=False, debug=False)

    def din(name, shape, dtype=F32):
        return nc.dram_tensor(name, shape, dtype, kind="ExternalInput").ap()

    xT = din("xT", [C, HW])
    WHL = [4, 128, 2048]
    wk = din("wk", WHL); wv = din("wv", WHL); wr = din("wr", WHL)
    wo = din("wo", WHL); wcr = din("wcr", WHL)
    wck = din("wck", [4] + WHL)
    wcv = din("wcv", [4] + WHL)
    router = din("router", [C, 16])
    moe_gu = din("moe_gu", [E_MOE] + WHL)
    moe_dn = din("moe_dn", [E_MOE, F_EXP, C])
    sh_gu = din("sh_gu", [C, 2048])
    sh_dn = din("sh_dn", [C, C])
    w1 = din("w1", WHL)
    pv = {}
    for p in ("ln1_g", "ln1_b", "ln2_g", "ln2_b", "td", "tf",
              "tmk", "tmv", "tmr", "cmk", "cmr", "qs", "ks", "gamma", "beta"):
        pv[p] = din(p, [C])
    dl = din("dl", [128, 1]); osc = din("osc", [128, 1])
    valid = din("valid", [128, 1])
    wkv_pow = din("wkv_pow", [128, 64]); wkv_mask = din("wkv_mask", [128, 64])
    l1_pow = din("l1_pow", [128, 8]); l1_mask = din("l1_mask", [128, 8])
    s_tok_hbm_decl = None
    ident_in = din("ident_in", [128, 128])
    iota128_in = din("iota128_in", [128, 1])
    iota16_in = din("iota16_in", [128, 16])
    oh16_in = din("oh16_in", [16, 16, 128])
    et_in = din("et_in", [128, 4, 512])

    out = nc.dram_tensor("out", [C, N], F32, kind="ExternalOutput").ap()
    tap_aps = {}
    for tname, tshape, tdt in taps:
        tap_aps[tname] = nc.dram_tensor(tname, tshape, tdt, kind="ExternalOutput").ap()

    g_hbm = nc.dram_tensor("g_hbm", [C, N], BF16).ap()
    sr_hbm = nc.dram_tensor("sr_hbm", [C, N], BF16).ap()
    sr2_hbm = nc.dram_tensor("sr2_hbm", [C, N], BF16).ap()
    s_tok_hbm = nc.dram_tensor("s_tok_hbm", [N, C], BF16).ap()
    stat4_hbm = nc.dram_tensor("stat4_hbm", [2, HW], F32).ap()
    sred = nc.dram_tensor("sred", [4, 128], F32).ap()
    cc1_in = nc.dram_tensor("cc1_in", [128, 16], F32)
    cc1_out = nc.dram_tensor("cc1_out", [128 * NCORE, 16], F32)
    cc2_in = nc.dram_tensor("cc2_in", [C, C], BF16)
    cc2_out = nc.dram_tensor("cc2_out", [C * NCORE, C], BF16)
    cc3_in = nc.dram_tensor("cc3_in", [1, 128], F32)
    cc3_out = nc.dram_tensor("cc3_out", [NCORE, 128], F32)

    with contextlib.ExitStack() as ctx:
        E = ctx.enter_context

        bfA = E(nc.sbuf_tensor("bfA", [128, G, HW], BF16))
        bfB = E(nc.sbuf_tensor("bfB", [128, G, HW], BF16))
        bfC = E(nc.sbuf_tensor("bfC", [128, G, HW], BF16))
        bfD = E(nc.sbuf_tensor("bfD", [128, G, HW], BF16))
        wpool = E(nc.sbuf_tensor("wpool", [128, G, 2048], BF16))

        def wg(h, kc):
            return wpool[:, h * 4 + kc // 2, (kc % 2) * 1024:(kc % 2) * 1024 + 1024]
        bmu = E(nc.sbuf_tensor("bmu", [128, HW], F32))
        brs = E(nc.sbuf_tensor("brs", [128, HW], F32))
        hscr = E(nc.sbuf_tensor("hscr", [128, HW], F32))
        pcol = E(nc.sbuf_tensor("pcol", [128, 24, G], F32))
        ones_bf = E(nc.sbuf_tensor("ones_bf", [128, 4], BF16))
        ones_row = E(nc.sbuf_tensor("ones_row", [1, 128], BF16))
        smal = E(nc.sbuf_tensor("smal", [128, 96], F32))
        iota16sb = E(nc.sbuf_tensor("iota16sb", [128, 16], F32))
        ident = E(nc.sbuf_tensor("ident", [128, 128], BF16))
        rt_sb = E(nc.sbuf_tensor("rt_sb", [128, 8, 16], BF16))
        oh16 = E(nc.sbuf_tensor("oh16", [16, 16, 128], BF16))
        rtok = E(nc.sbuf_tensor("rtok", [128, 16, 16], F32))
        rsc = E(nc.sbuf_tensor("rsc", [128, 16], F32))
        rsc3 = E(nc.sbuf_tensor("rsc3", [128, 4, 16], F32))
        rtk2 = E(nc.sbuf_tensor("rtk2", [128, 16, 16], F32))
        ctok = E(nc.sbuf_tensor("ctok", [128, 16, 16], BF16))
        bcoe = E(nc.sbuf_tensor("bcoe", [128, 1024], BF16))
        sscr = E(nc.sbuf_tensor("sscr", [128, 4, 512], BF16))
        srg = E(nc.sbuf_tensor("srg", [128, HW], BF16))
        cfm = srg[0:16, 0:2048]
        lsb = sscr[0:16, 0, 0:512]

        ps = [E(nc.psum_tensor(f"ps{i}", [128, 512], F32)) for i in range(6)]
        pT = [E(nc.psum_tensor(f"pT{i}", [128, 256], BF16)) for i in range(2)]

        tsem = Sem(E(nc.semaphore("tsem")), "tsem", 1)
        vsem = Sem(E(nc.semaphore("vsem")), "vsem", 1)
        ssem = Sem(E(nc.semaphore("ssem")), "ssem", 1)
        gsem = Sem(E(nc.semaphore("gsem")), "gsem", 1)
        dg = Sem(E(nc.semaphore("dg")), "dg", 16)
        ds = Sem(E(nc.semaphore("ds")), "ds", 16)
        dc = Sem(E(nc.semaphore("dc")), "dc", 16)
        csem = Sem(E(nc.semaphore("csem")), "csem", 1)

        plan = Plan()
        SEMS = dict(t=tsem, v=vsem, s=ssem, g=gsem)

        def dma(queue, dst, src, sem=None, slow=False):
            sem = sem or (dg if queue == "gpsimd" else ds if queue == "sync" else dc)
            return op(plan, queue,
                      lambda e, d=dst, s=src, sl=slow: e.dma_start(
                          out=d, in_=s, allow_slow_non_contiguous=sl),
                      inc=sem)

        def ew(eng, fn, deps=(), reduce=False):
            for sem, cnt in deps:
                wait(plan, eng, sem, cnt)
            c = op(plan, eng, fn, inc=None if reduce else (SEMS[eng[0]] if eng != "sync" else None))
            if reduce:
                for _ in range(3):
                    op(plan, eng, lambda e: e.engine_nop())
                c = op(plan, eng, lambda e: e.engine_nop(), inc=SEMS[eng[0]])
            return c

        class Bank:
            def __init__(self, apt):
                self.ap = apt
                self.free_sem = None

        banks = [Bank(p) for p in ps]
        bank_rr = [0]

        def get_bank():
            b = banks[bank_rr[0] % len(banks)]
            bank_rr[0] += 1
            return b

        def mm_group(bank, pairs, out_slice=None, deps=()):
            if bank.free_sem is not None:
                wait(plan, "tensor", *bank.free_sem)
            for sem, cnt in deps:
                wait(plan, "tensor", sem, cnt)
            tgt = out_slice if out_slice is not None else bank.ap[:]
            nprs = len(pairs)
            cnt = None
            for i, (l, r) in enumerate(pairs):
                last = i == nprs - 1
                c = op(plan, "tensor",
                       lambda e, l=l, r=r, s=(i == 0), st=last, tg=tgt:
                       e.matmul(tg, l, r, start=s, stop=st),
                       inc=tsem if last else None)
                if last:
                    cnt = c
            return cnt

        PIDX = {}

        def pc(name):
            return pcol[:, PIDX[name], :]

        def pcg(name, g):
            return pcol[:, PIDX[name], g:g + 1]

        for i, name in enumerate(("ln1_g", "ln1_b", "ln2_g", "ln2_b", "td", "tf",
                                  "tmk", "tmv", "tmr", "cmk", "cmr", "qs", "ks",
                                  "gamma", "beta")):
            PIDX[name] = i
            dma("gpsimd", pcol[:, i, :],
                pv[name].rearrange("(g p) -> p g", p=128), slow=True)
        n_par_dma = len(PIDX)
        dma("gpsimd", ident[:], ident_in)
        dma("gpsimd", smal[:, 0:1], valid)
        dma("gpsimd", rtok[:, 0:4, :], wkv_pow.rearrange("p (a b) -> p a b", b=16))
        dma("gpsimd", rtok[:, 4:8, :], wkv_mask.rearrange("p (a b) -> p a b", b=16))
        dma("gpsimd", smal[:, 17:25], l1_pow)
        dma("gpsimd", smal[:, 25:33], l1_mask)
        dma("gpsimd", smal[:, 89:90], iota128_in)
        dma("gpsimd", iota16sb[:], iota16_in)
        dma("gpsimd", smal[:, 33:34], dl)
        dma("gpsimd", smal[:, 34:35], osc)
        c_par = dg.n
        ew("vector", lambda e: e.memset(ones_bf[:, 0:1], 1.0))
        ew("vector", lambda e: e.memset(ones_bf[:, 1:2], 1.0 / C))
        ew("vector", lambda e: e.memset(ones_row[:], 1.0))
        ew("vector", lambda e: e.memset(smal[:, 88:89], EPS))
        nc.const_aps.register(F32, EPS, smal[:, 88:89])
        c_ones = vsem.n
        wait(plan, "scalar", vsem, c_ones)
        COL_VALID, COL_WPOW, COL_WMASK, COL_LPOW, COL_LMASK = 0, 1, 9, 17, 25
        COL_DL, COL_OSC = 33, 34
        COL_EW, COL_EU = 35, 43
        COL_LND, COL_IND, COL_SC1, COL_SC2 = 51, 52, 53, 54
        COL_W, COL_W2048, COL_CA, COL_CB = 56, 64, 72, 80

        x_re = xT.rearrange("(g p) t -> p g t", p=128)
        c_xbf = dma("gpsimd", bfB[:], x_re)

        wait(plan, "scalar", dg, c_par)
        ew("scalar", lambda e: e.activation(hscr[:, 0:8], pc("td"), AF.Exp))
        ew("vector", lambda e: e.tensor_scalar_mul(smal[:, COL_W:COL_W + 8], hscr[:, 0:8], -1.0),
           deps=[(ssem, ssem.n)])
        c_w = vsem.n
        ew("scalar", lambda e: e.activation(smal[:, COL_EW:COL_EW + 8],
                                            smal[:, COL_W:COL_W + 8], AF.Exp),
           deps=[(vsem, c_w)])
        ew("scalar", lambda e: e.activation(smal[:, COL_EU:COL_EU + 8], pc("tf"), AF.Exp))
        ew("vector", lambda e: e.tensor_scalar_mul(smal[:, COL_W2048:COL_W2048 + 8],
                                                   smal[:, COL_W:COL_W + 8], 2048.0),
           deps=[(vsem, c_w)])
        for nm, a_, b_ in (("d_rk", "tmr", "tmk"), ("d_vr", "tmv", "tmr"),
                           ("d_rk2", "cmr", "cmk")):
            PIDX[nm] = len(PIDX)
            ew("vector", lambda e, nm=nm, a_=a_, b_=b_: e.tensor_tensor(
                pc(nm), pc(a_), pc(b_), ALU.subtract), deps=[(dg, c_par)])
        ew("scalar", lambda e: e.activation(hscr[:, 24:25], smal[:, COL_DL:COL_DL + 1],
                                            AF.Exp, scale=-1.0))
        ew("vector", lambda e: e.tensor_scalar_add(hscr[:, 25:26], hscr[:, 24:25], 1.0),
           deps=[(ssem, ssem.n)])
        ew("scalar", lambda e: e.activation(hscr[:, 24:25], hscr[:, 25:26], AF.Ln),
           deps=[(vsem, vsem.n)])
        ew("vector", lambda e: e.tensor_scalar_mul(smal[:, COL_LND:COL_LND + 1],
                                                   hscr[:, 24:25], -1.0),
           deps=[(ssem, ssem.n)])
        ew("vector", lambda e: e.memset(hscr[:, 25:26], 0.5), reduce=True)
        ew("vector", lambda e: e.tensor_scalar(smal[:, COL_IND:COL_IND + 1], hscr[:, 25:26],
                                               C32, C32, ALU.add, ALU.subtract),
           reduce=True)
        ew("vector", lambda e: e.tensor_scalar(smal[:, COL_SC1:COL_SC1 + 1],
                                               smal[:, COL_IND:COL_IND + 1],
                                               -2.0, 1.0, ALU.mult, ALU.add),
           reduce=True)
        ew("vector", lambda e: e.tensor_scalar_mul(smal[:, COL_SC2:COL_SC2 + 1],
                                                   smal[:, COL_IND:COL_IND + 1], 2.0),
           reduce=True)
        c_derived_v = vsem.n
        c_derived_s = ssem.n

        def ln_stats(src, sq_arena, src_ready, extra_deps=()):
            sq_cnts = []
            for g in range(G):
                c = ew("scalar", lambda e, g=g: e.activation(
                    sq_arena[:, g, 0:NCOL], src[:, g, 0:NCOL], AF.Square),
                    deps=([src_ready] + list(extra_deps)) if g == 0 else [])
                sq_cnts.append(c)
            for si, (o, w) in enumerate(SPANS):
                bk1 = get_bank()
                c1 = mm_group(bk1, [(ones_bf[:, 1:2], src[:, g, o:o + w]) for g in range(G)],
                              out_slice=bk1.ap[0:1, 0:w],
                              deps=[src_ready])
                ec1 = ew("scalar", lambda e, b=bk1, o=o, w=w: e.copy(
                    hscr[0:1, o:o + w], b.ap[0:1, 0:w]), deps=[(tsem, c1)])
                bk1.free_sem = (ssem, ec1)
                bk2 = get_bank()
                c2 = mm_group(bk2, [(ones_bf[:, 1:2], sq_arena[:, g, o:o + w]) for g in range(G)],
                              out_slice=bk2.ap[0:1, 0:w],
                              deps=[(SEMS["s"], sq_cnts[-1])])
                ec2 = ew("scalar", lambda e, b=bk2, o=o, w=w: e.copy(
                    hscr[32:33, o:o + w], b.ap[0:1, 0:w]), deps=[(tsem, c2)])
                bk2.free_sem = (ssem, ec2)
            c_rows = ssem.n
            ew("scalar", lambda e: e.copy(sq_arena[0:1, 0, 0:NCOL], hscr[0:1, 0:NCOL]),
               deps=[(ssem, c_rows)])
            c_rowa = ew("scalar", lambda e: e.copy(sq_arena[0:1, 1, 0:NCOL], hscr[32:33, 0:NCOL]))
            for si, (o, w) in enumerate(SPANS):
                bk = get_bank()
                cm = mm_group(bk, [(ones_row[:], sq_arena[0:1, 0, o:o + w])],
                              out_slice=bk.ap[:, 0:w],
                              deps=[(ssem, c_rowa), (vsem, c_ones)])
                ecm = ew("scalar", lambda e, b=bk, o=o, w=w: e.copy(
                    bmu[:, o:o + w], b.ap[:, 0:w]), deps=[(tsem, cm)])
                bk.free_sem = (ssem, ecm)
                bk2 = get_bank()
                cm2 = mm_group(bk2, [(ones_row[:], sq_arena[0:1, 1, o:o + w])],
                               out_slice=bk2.ap[:, 0:w],
                               deps=[(ssem, c_rowa)])
                ecm2 = ew("scalar", lambda e, b=bk2, o=o, w=w: e.copy(
                    brs[:, o:o + w], b.ap[:, 0:w]), deps=[(tsem, cm2)])
                bk2.free_sem = (ssem, ecm2)
            c_bc = ssem.n
            ew("vector", lambda e: e.tensor_tensor(hscr[:, 0:NCOL], bmu[:, 0:NCOL],
                                                   bmu[:, 0:NCOL], ALU.mult),
               deps=[(ssem, c_bc)])
            ew("vector", lambda e: e.tensor_tensor(brs[:, 0:NCOL], brs[:, 0:NCOL],
                                                   hscr[:, 0:NCOL], ALU.subtract))
            cv = vsem.n
            ew("scalar", lambda e: e.activation(brs[:, 0:NCOL], brs[:, 0:NCOL], AF.Ln,
                                                bias=EPS), deps=[(vsem, cv)])
            cs = ew("scalar", lambda e: e.activation(brs[:, 0:NCOL], brs[:, 0:NCOL], AF.Exp,
                                                     scale=-0.5))
            return (ssem, cs), (ssem, c_rowa)

        stats_ready, hscr_free = ln_stats(bfB, bfA, (dg, c_xbf))
        wait(plan, "sync", stats_ready[0], stats_ready[1])
        wait(plan, "sync", vsem, vsem.n)
        c_xn_chunks = []
        for g in range(G):
            cx = dma("sync", hscr[:, 0:HW], x_re[:, g, :])
            c1 = ew("vector", lambda e, g=g: e.tensor_tensor(
                hscr[:, 0:NCOL], hscr[:, 0:NCOL], bmu[:, 0:NCOL], ALU.subtract),
                deps=[(ds, cx), stats_ready])
            c2 = ew("vector", lambda e, g=g: e.tensor_tensor(
                bfA[:, g, 0:NCOL], hscr[:, 0:NCOL], brs[:, 0:NCOL], ALU.mult))
            c3 = ew("vector", lambda e, g=g: e.tensor_scalar_mul(
                bfA[:, g, 0:2], bfA[:, g, 0:2], smal[:, COL_VALID:COL_VALID + 1]),
                deps=[(dg, c_par)])
            c_xn_chunks.append(c3)
            wait(plan, "sync", vsem, c2)
        c_xn = vsem.n

        def slot16(arena, j):
            return arena[:, j // 2, (j % 2) * 1024:(j % 2) * 1024 + 1024]

        c_shgu = dma("gpsimd", wpool[:], sh_gu.rearrange("(kc p) f -> p kc f", p=128))
        c_rt = dma("gpsimd", rt_sb[:], router.rearrange("(kc p) e -> p kc e", p=128),
                   slow=True)
        c_oh16 = dma("gpsimd", oh16[:], oh16_in)
        c_shdn = dma("gpsimd", bfD[:, :, 0:1024],
                     sh_dn.rearrange("(fc p) c -> p fc c", p=128))

        pT_free = [0, 0]
        for sp in range(4):
            o = C0 + 512 * sp
            bk = get_bank()
            cl = mm_group(bk, [(rt_sb[:, kc, :], bfA[:, kc, o:o + 512]) for kc in range(G)],
                          out_slice=bk.ap[0:16, 0:512],
                          deps=[(dg, c_rt), (vsem, c_xn)])
            ce = ew("scalar", lambda e, b=bk: e.copy(lsb[:], b.ap[0:16, 0:512]),
                    deps=[(tsem, cl)])
            bk.free_sem = (ssem, ce)
            for j in range(4):
                tile = 4 * sp + j
                par = tile % 2
                wait(plan, "tensor", ssem, ce)
                wait(plan, "tensor", vsem, pT_free[par])
                ct = op(plan, "tensor",
                        lambda e, j=j, pt=pT[par]: e.transpose(
                            pt[:, 0:16], lsb[0:16, j * 128:(j + 1) * 128],
                            ident[0:16, 0:16]),
                        inc=tsem)
                cc_ = ew("vector", lambda e, tile=tile, pt=pT[par]: e.tensor_copy(
                    rtok[:, tile, :], pt[:, 0:16]), deps=[(tsem, ct)])
                pT_free[par] = cc_
        c_rtok = vsem.n
        def bc16(col):
            return rsc3[:, col:col + 1, :].rearrange("p a b -> p b a").broadcast_to([128, 16, 16])

        ew("vector", lambda e: e.reduce_max(rsc3[:, 0, :], rtok[:], axis=AX.X),
           deps=[(vsem, c_rtok)], reduce=True)
        ew("vector", lambda e: e.tensor_tensor(rtk2[:], rtok[:], bc16(0), ALU.subtract))
        cvs = vsem.n
        cexp = ew("scalar", lambda e: e.activation(rtk2[:], rtk2[:], AF.Exp),
                  deps=[(vsem, cvs)])
        ew("vector", lambda e: e.reduce_sum(rsc3[:, 1, :], rtk2[:], axis=AX.X),
           deps=[(ssem, cexp)], reduce=True)
        ew("vector", lambda e: e.reciprocal(rsc3[:, 2, :], rsc3[:, 1, :]), reduce=True)
        ew("vector", lambda e: e.tensor_tensor(rtok[:], rtk2[:], bc16(2), ALU.mult))
        ew("vector", lambda e: e.reduce_max(rsc3[:, 0, :], rtok[:], axis=AX.X), reduce=True)
        ew("vector", lambda e: e.tensor_tensor(rtk2[:], rtok[:], bc16(0), ALU.is_lt))
        ew("vector", lambda e: e.tensor_tensor(rtk2[:], rtok[:], rtk2[:], ALU.mult))
        ew("vector", lambda e: e.reduce_max(rsc3[:, 1, :], rtk2[:], axis=AX.X), reduce=True)
        ew("vector", lambda e: e.tensor_tensor(rtk2[:], rtok[:], bc16(1), ALU.is_ge))
        ew("vector", lambda e: e.tensor_tensor(ctok[:], rtok[:], rtk2[:], ALU.mult))
        c_ctok = vsem.n
        if "tap_rtok" in tap_aps:
            wait(plan, "gpsimd", vsem, c_ctok)
            dma("gpsimd", tap_aps["tap_rtok"], rtok[:].rearrange("p a b -> p (a b)"))
        if "tap_ctok" in tap_aps:
            wait(plan, "gpsimd", vsem, c_ctok)
            dma("gpsimd", tap_aps["tap_ctok"], ctok[:].rearrange("p a b -> p (a b)"))
        for tile in range(16):
            par = tile % 2
            wait(plan, "tensor", vsem, max(c_ctok, pT_free[par]))
            ct = op(plan, "tensor",
                    lambda e, tile=tile, pt=pT[par]: e.transpose(
                        pt[0:16, 0:128], ctok[:, tile, :], ident[:]),
                    inc=tsem)
            cc_ = ew("vector", lambda e, tile=tile, pt=pT[par]: e.tensor_copy(
                cfm[:, tile * 128:(tile + 1) * 128], pt[0:16, 0:128]),
                deps=[(tsem, ct)])
            pT_free[par] = cc_
        c_cfm = vsem.n
        if "tap_cfm" in tap_aps:
            wait(plan, "gpsimd", vsem, c_cfm)
            dma("gpsimd", tap_aps["tap_cfm"], cfm[:])

        sscr_free = [0, 0, 0, 0]
        for fg in range(G):
            for sp in range(4):
                o = C0 + 512 * sp
                bkg = get_bank()
                cg = mm_group(bkg, [(wpool[:, kc, fg * 128:(fg + 1) * 128],
                                     bfA[:, kc, o:o + 512]) for kc in range(G)],
                              deps=[(dg, c_shgu), (vsem, c_xn)])
                bku = get_bank()
                cu = mm_group(bku, [(wpool[:, kc, 1024 + fg * 128:1024 + (fg + 1) * 128],
                                     bfA[:, kc, o:o + 512]) for kc in range(G)])
                csl = ew("scalar", lambda e, b=bkg, sl=sp % 2: e.activation(
                    sscr[:, sl, :], b.ap[:], AF.Silu),
                    deps=[(tsem, cg), (vsem, sscr_free[sp % 2])])
                bkg.free_sem = (ssem, csl)
                cin = ew("vector", lambda e, b=bku, fg=fg, sp=sp, sl=sp % 2: e.tensor_tensor(
                    bfB[:, fg, sp * 512:(sp + 1) * 512], sscr[:, sl, :],
                    b.ap[:], ALU.mult), deps=[(ssem, csl), (tsem, cu)])
                bku.free_sem = (vsem, cin)
                sscr_free[sp % 2] = cin
        c_inter_s = vsem.n
        for cc in range(G):
            for sp in range(4):
                bk = get_bank()
                cd = mm_group(bk, [(bfD[:, fc, cc * 128:(cc + 1) * 128],
                                    bfB[:, fc, sp * 512:(sp + 1) * 512]) for fc in range(G)],
                              deps=[(dg, c_shdn), (vsem, c_inter_s)])
                ce = ew("scalar", lambda e, b=bk, cc=cc, sp=sp: e.copy(
                    bfC[:, cc, C0 + sp * 512:C0 + (sp + 1) * 512], b.ap[:]),
                    deps=[(tsem, cd)])
                bk.free_sem = (ssem, ce)
        c_g_shared = ssem.n
        c_shared_mm_done = tsem.n
        if "tap_gsh" in tap_aps:
            wait(plan, "gpsimd", ssem, c_g_shared)
            for cc in range(G):
                dma("gpsimd", tap_aps["tap_gsh"].rearrange("(g p) t -> p g t", p=128)[:, cc, :],
                    bfC[:, cc, C0:C0 + N])

        gu_load = [None, None]
        gu_free = [c_shared_mm_done, c_shared_mm_done]
        c_down_prev = c_shared_mm_done
        prev_cm1 = 0
        for grp in range(4):
            for half in range(2):
                ho = C0 + half * 1024
                wait(plan, "gpsimd", tsem, c_down_prev)
                dn_cnt = None
                for el in range(4):
                    e_id = grp * 4 + el
                    src4 = moe_dn[e_id].rearrange("(fg p) c -> p fg c", p=128)
                    for fg in range(4):
                        dn_cnt = dma("gpsimd", slot16(bfD, el * 4 + fg), src4[:, fg, :])
                last_v2 = 0
                for el in range(4):
                    e_id = grp * 4 + el
                    par = e_id % 2
                    wait(plan, "gpsimd", tsem, gu_free[par])
                    gu_load[par] = dma(
                        "gpsimd", wpool[:, par * 4:(par + 1) * 4, 0:2048],
                        moe_gu[e_id].rearrange("a p f -> p a f"))
                    c_be = 0
                    for bsp in range(2):
                        bkb = get_bank()
                        cb = mm_group(bkb, [(oh16[:, e_id, :],
                                             cfm[:, half * 1024 + bsp * 512:
                                                 half * 1024 + (bsp + 1) * 512])],
                                      deps=[(vsem, c_cfm), (dg, c_oh16)])
                        c_be = ew("scalar", lambda e, b=bkb, bsp=bsp: e.copy(
                            bcoe[:, bsp * 512:(bsp + 1) * 512], b.ap[:]),
                            deps=[(tsem, cb), (gsem, prev_cm1)])
                        bkb.free_sem = (ssem, c_be)
                    for fg in range(4):
                        for sp in range(2):
                            o = ho + sp * 512
                            bkg = get_bank()
                            cg = mm_group(bkg, [(wg(par, kc)[:, fg * 128:(fg + 1) * 128],
                                                 bfA[:, kc, o:o + 512]) for kc in range(G)],
                                          deps=[(dg, gu_load[par])])
                            bku = get_bank()
                            cu = mm_group(bku, [(wg(par, kc)[:, 512 + fg * 128:
                                                              512 + (fg + 1) * 128],
                                                 bfA[:, kc, o:o + 512]) for kc in range(G)])
                            gu_free[par] = cu
                            csl = ew("scalar", lambda e, b=bkg, sl=sp: e.activation(
                                sscr[:, sl, :], b.ap[:], AF.Silu),
                                deps=[(tsem, cg), (vsem, sscr_free[sp])])
                            bkg.free_sem = (ssem, csl)
                            cue = ew("scalar", lambda e, b=bku, sl=2 + sp: e.copy(
                                sscr[:, sl, :], b.ap[:]),
                                deps=[(tsem, cu), (vsem, sscr_free[2 + sp])])
                            bku.free_sem = (ssem, cue)
                            cm1 = ew("gpsimd", lambda e, sl=2 + sp, bsp=sp: e.tensor_tensor(
                                sscr[:, sl, :], sscr[:, sl, :],
                                bcoe[:, bsp * 512:(bsp + 1) * 512], ALU.mult),
                                deps=[(ssem, max(cue, c_be))])
                            prev_cm1 = cm1
                            sl16 = slot16(bfB, el * 4 + fg)
                            cm2 = ew("vector", lambda e, sl16=sl16, sp=sp, sg=sp, su=2 + sp:
                                     e.tensor_tensor(
                                         sl16[:, sp * 512:(sp + 1) * 512], sscr[:, sg, :],
                                         sscr[:, su, :], ALU.mult),
                                     deps=[(gsem, cm1), (ssem, csl), (tsem, c_down_prev)])
                            last_v2 = cm2
                            sscr_free[sp] = cm2
                            sscr_free[2 + sp] = cm2
                cdown = None
                for cc in range(G):
                    for sp in range(2):
                        bk = get_bank()
                        pairs = []
                        for el in range(4):
                            for fg in range(4):
                                sl16 = slot16(bfB, el * 4 + fg)
                                dnsl = slot16(bfD, el * 4 + fg)
                                pairs.append((dnsl[:, cc * 128:(cc + 1) * 128],
                                              sl16[:, sp * 512:(sp + 1) * 512]))
                        cdown = mm_group(bk, pairs,
                                         deps=[(vsem, last_v2), (dg, dn_cnt)])
                        cga = ew("vector", lambda e, b=bk, cc=cc, sp=sp, ho=ho:
                                 e.tensor_tensor(
                                     bfC[:, cc, ho + sp * 512:ho + (sp + 1) * 512],
                                     bfC[:, cc, ho + sp * 512:ho + (sp + 1) * 512],
                                     b.ap[:], ALU.add),
                                 deps=[(tsem, cdown), (ssem, c_g_shared)])
                        bk.free_sem = (vsem, cga)
                c_down_prev = cdown
        c_g_adds = vsem.n
        c_gsilu = 0
        for cc in range(G):
            c_gsilu = ew("scalar", lambda e, cc=cc: e.activation(
                bfC[:, cc, C0:C0 + N], bfC[:, cc, C0:C0 + N], AF.Silu),
                deps=[(vsem, c_g_adds)])
        wait(plan, "sync", ssem, c_gsilu)
        c_ghbm = None
        for cc in range(G):
            c_ghbm = dma("sync", g_hbm.rearrange("(g p) t -> p g t", p=128)[:, cc, :],
                         bfC[:, cc, C0:C0 + N])
        if "tap_g" in tap_aps:
            wait(plan, "gpsimd", ssem, c_gsilu)
            for cc in range(G):
                dma("gpsimd", tap_aps["tap_g"].rearrange("(g p) t -> p g t", p=128)[:, cc, :],
                    bfC[:, cc, C0:C0 + N])

        def loadW(dram_t, half, after_tsem):
            wait(plan, "gpsimd", tsem, after_tsem)
            return dma("gpsimd", wpool[:, half * 4:(half + 1) * 4, 0:2048],
                       dram_t.rearrange("a p f -> p a f"))

        c_moe_mm_done = tsem.n
        c_wk = loadW(wk, 0, 0)
        stats2_ready, _ = ln_stats(bfA, bfB, (vsem, c_g_adds),
                                   extra_deps=[(tsem, c_moe_mm_done)])
        c_dn_done = c_down_prev
        c_xk = 0
        for gch in range(G):
            ch1 = ew("vector", lambda e, g=gch: e.tensor_tensor(
                hscr[:, 0:NCOL], bfA[:, g, 0:NCOL], bmu[:, 0:NCOL], ALU.subtract),
                deps=[stats2_ready, (dg, c_par)])
            ew("vector", lambda e: e.tensor_tensor(
                hscr[:, 0:NCOL], hscr[:, 0:NCOL], brs[:, 0:NCOL], ALU.mult))
            ew("vector", lambda e, g=gch: e.tensor_scalar(
                hscr[:, 0:NCOL], hscr[:, 0:NCOL], pcg("ln1_g", g), pcg("ln1_b", g),
                ALU.mult, ALU.add))
            ew("vector", lambda e: e.tensor_scalar_mul(
                hscr[:, 0:2], hscr[:, 0:2], smal[:, COL_VALID:COL_VALID + 1]), reduce=True)
            ew("vector", lambda e, g=gch: e.tensor_tensor(
                bfD[:, g, 1:NCOL], hscr[:, 1:NCOL], hscr[:, 0:NCOL - 1], ALU.subtract),
                deps=[(tsem, c_dn_done)])
            c_xk = ew("vector", lambda e, g=gch: e.scalar_tensor_tensor(
                bfB[:, g, 1:NCOL], bfD[:, g, 1:NCOL], pcg("tmk", g),
                hscr[:, 0:NCOL - 1], ALU.mult, ALU.add))
        if "tap_xk" in tap_aps:
            wait(plan, "gpsimd", vsem, c_xk)
            for gch in range(G):
                dma("gpsimd", tap_aps["tap_xk"].rearrange("(g p) t -> p g t", p=128)[:, gch, :],
                    bfB[:, gch, 0:HW])
        if "tap_bmu" in tap_aps:
            wait(plan, "gpsimd", ssem, stats2_ready[1])
            dma("gpsimd", tap_aps["tap_bmu"], bmu[:])
            dma("gpsimd", tap_aps["tap_brs"], brs[:])
        def proj_pass(rhs_arena, whalf, out_fn, wload, extra_deps=()):
            last = 0
            for oc in range(G):
                for (o, w_) in SPANS:
                    bk = get_bank()
                    cm = mm_group(bk, [(wg(whalf, kc)[:, oc * 128:(oc + 1) * 128],
                                        rhs_arena[:, kc, o:o + w_]) for kc in range(G)],
                                  out_slice=bk.ap[:, 0:w_],
                                  deps=[(dg, wload)] + list(extra_deps))
                    last = out_fn(oc, o, w_, bk, cm)
            return last

        wait(plan, "tensor", vsem, c_xk)
        wait(plan, "tensor", ssem, c_g_shared)
        wait(plan, "tensor", ds, c_ghbm)

        def ek_evac(oc, o, w_, bk, cm):
            c = ew("scalar", lambda e, oc=oc, o=o, w_=w_, b=bk: e.activation(
                bfC[:, oc, o:o + w_], b.ap[:, 0:w_], AF.Exp), deps=[(tsem, cm)])
            bk.free_sem = (ssem, c)
            return c
        c_ek = proj_pass(bfB, 0, ek_evac, c_wk, extra_deps=[(vsem, c_xk)])
        if "tap_ek" in tap_aps:
            wait(plan, "gpsimd", ssem, c_ek)
            for gch in range(G):
                dma("gpsimd", tap_aps["tap_ek"].rearrange("(g p) t -> p g t", p=128)[:, gch, :],
                    bfC[:, gch, 0:HW])
        c_kmm_done = tsem.n
        c_wr = loadW(wr, 1, 0)
        c_xr = 0
        for gch in range(G):
            c_xr = ew("vector", lambda e, g=gch: e.scalar_tensor_tensor(
                bfB[:, g, 1:NCOL], bfD[:, g, 1:NCOL], pcg("d_rk", g),
                bfB[:, g, 1:NCOL], ALU.mult, ALU.add), deps=[(tsem, c_kmm_done)])
        sr_hbm_re = sr_hbm.rearrange("(g p) t -> p g t", p=128)
        c_sr_dma = [None]
        sig_done = {}

        def sr_evac(oc, o, w_, bk, cm):
            c = ew("scalar", lambda e, o=o, w_=w_, b=bk: e.activation(
                srg[:, o:o + w_], b.ap[:, 0:w_], AF.Sigmoid),
                deps=[(tsem, cm)] + ([(ds, c_sr_dma[0])] if (o == 0 and c_sr_dma[0]) else []))
            bk.free_sem = (ssem, c)
            sig_done[oc] = c
            return c

        def sr_flush(oc):
            ew("vector", lambda e, oc=oc: e.tensor_copy(rsc3[:, 3, oc:oc + 1],
                                                        srg[:, 1:2]),
               deps=[(ssem, sig_done[oc])], reduce=True)
            wait(plan, "sync", ssem, sig_done[oc])
            c_sr_dma[0] = dma("sync", sr_hbm_re[:, oc, :], srg[:, C0:C0 + N])

        last = 0
        for oc in range(G):
            for (o, w_) in SPANS:
                bk = get_bank()
                cm = mm_group(bk, [(wg(1, kc)[:, oc * 128:(oc + 1) * 128],
                                    bfB[:, kc, o:o + w_]) for kc in range(G)],
                              out_slice=bk.ap[:, 0:w_],
                              deps=[(dg, c_wr), (vsem, c_xr)])
                sr_evac(oc, o, w_, bk, cm)
            sr_flush(oc)
        c_rmm_done = tsem.n
        c_wv = loadW(wv, 0, c_kmm_done)
        c_xv = 0
        for gch in range(G):
            c_xv = ew("vector", lambda e, g=gch: e.scalar_tensor_tensor(
                bfB[:, g, 1:NCOL], bfD[:, g, 1:NCOL], pcg("d_vr", g),
                bfB[:, g, 1:NCOL], ALU.mult, ALU.add), deps=[(tsem, c_rmm_done)])
        def ekv_evac(oc, o, w_, bk, cm):
            c = ew("vector", lambda e, oc=oc, o=o, w_=w_, b=bk: e.tensor_tensor(
                bfD[:, oc, o:o + w_], bfC[:, oc, o:o + w_], b.ap[:, 0:w_], ALU.mult),
                deps=[(tsem, cm), (ssem, c_ek)])
            bk.free_sem = (vsem, c)
            return c
        c_ekv = proj_pass(bfB, 0, ekv_evac, c_wv, extra_deps=[(vsem, c_xv)])
        scn = wpool[:, 0:2, 0:2048]
        if "tap_ekv" in tap_aps:
            wait(plan, "gpsimd", vsem, c_ekv)
            for gch in range(G):
                dma("gpsimd", tap_aps["tap_ekv"].rearrange("(g p) t -> p g t", p=128)[:, gch, :],
                    bfD[:, gch, 0:HW])
        c_hmask = 0
        for gch in range(G):
            ew("vector", lambda e, g=gch: e.tensor_copy(rsc3[:, 1, g:g + 1],
                                                        bfD[:, g, 1:2]),
               deps=[(vsem, c_ekv), (ssem, c_ek)], reduce=True)
            ew("vector", lambda e, g=gch: e.tensor_copy(rsc3[:, 2, g:g + 1],
                                                        bfC[:, g, 1:2]), reduce=True)
            ew("vector", lambda e, g=gch: e.tensor_scalar_mul(
                bfC[:, g, 0:2], bfC[:, g, 0:2], smal[:, COL_VALID:COL_VALID + 1]),
               reduce=True)
            c_hmask = ew("vector", lambda e, g=gch: e.tensor_scalar_mul(
                bfD[:, g, 0:2], bfD[:, g, 0:2], smal[:, COL_VALID:COL_VALID + 1]),
               reduce=True)
        c_s1 = 0
        for gch in range(G):
            ewb = smal[:, COL_EW + gch:COL_EW + gch + 1].broadcast_to([128, 2048])
            ew("vector", lambda e, g=gch, ewb=ewb: e.tensor_tensor_scan(
                scn[:, 0, :], ewb, bfD[:, g, 1:NCOL - 1], 0.0, ALU.mult, ALU.add),
                deps=[(vsem, c_hmask)], reduce=True)
            ew("vector", lambda e, g=gch, ewb=ewb: e.tensor_tensor_scan(
                scn[:, 1, :], ewb, bfC[:, g, 1:NCOL - 1], 0.0, ALU.mult, ALU.add),
               reduce=True)
            if gch == 0 and "tap_scn" in tap_aps:
                wait(plan, "gpsimd", vsem, vsem.n)
                dma("gpsimd", tap_aps["tap_scn"], scn[:, 0:2, :].rearrange("p a b -> p (a b)"))
            ew("vector", lambda e, g=gch: e.tensor_copy(rsc[:, g:g + 1],
                                                        scn[:, 0, 2047:2048]), reduce=True)
            c_s1 = ew("vector", lambda e, g=gch: e.tensor_copy(rsc[:, 8 + g:9 + g],
                                                               scn[:, 1, 2047:2048]),
                      reduce=True)
        wait(plan, "gpsimd", vsem, c_s1)
        c_cc1w = dma("gpsimd", cc1_in.ap(), rsc[:])
        wait(plan, "gpsimd", dg, c_cc1w)
        op(plan, "gpsimd", lambda e: e.collective_compute(
            "AllGather", ALU.bypass, replica_groups=[list(range(NCORE))],
            ins=[cc1_in.ap().opt()], outs=[cc1_out.ap().opt()]), inc=csem)
        c_cc1 = csem.n
        c_wo = loadW(wo, 1, c_rmm_done)
        wait(plan, "vector", csem, c_cc1)
        c_sg = 0
        for cc in range(NCORE):
            c_sg = dma("gpsimd", rtk2[:].rearrange("p a b -> p b a")[:, cc, :],
                       cc1_out.ap()[cc * 128:(cc + 1) * 128, :], slow=True)
        w2b = smal[:, COL_W2048:COL_W2048 + 8].rearrange("p (g o) -> p g o", o=1
            ).broadcast_to([128, 8, 8])
        pow3 = rtok[:, 0:4, :].rearrange("p a b -> p (a b)").rearrange(
            "p (g c) -> p g c", c=8)
        msk3 = rtok[:, 4:8, :].rearrange("p a b -> p (a b)").rearrange(
            "p (g c) -> p g c", c=8)
        ew("vector", lambda e: e.tensor_tensor(hscr[:, 0:64].rearrange(
            "p (g c) -> p g c", c=8), w2b, pow3, ALU.mult))
        cvv = vsem.n
        ces = ew("scalar", lambda e: e.activation(hscr[:, 64:128], hscr[:, 0:64], AF.Exp),
                 deps=[(vsem, cvv)])
        ew("vector", lambda e: e.tensor_tensor(hscr[:, 64:128].rearrange(
            "p (g c) -> p g c", c=8), hscr[:, 64:128].rearrange("p (g c) -> p g c", c=8),
            msk3, ALU.mult), deps=[(ssem, ces)])
        sA = rtk2[:, 0:8, 0:8]
        sB = rtk2[:, 8:16, 0:8]
        ew("vector", lambda e: e.tensor_tensor(hscr[:, 128:192].rearrange(
            "p (g c) -> p g c", c=8), hscr[:, 64:128].rearrange("p (g c) -> p g c", c=8),
            sA, ALU.mult), deps=[(dg, c_sg)])
        ew("vector", lambda e: e.reduce_sum(smal[:, COL_CA:COL_CA + 8],
                                            hscr[:, 128:192].rearrange(
                                                "p (g c) -> p g c", c=8), axis=AX.X),
           reduce=True)
        ew("vector", lambda e: e.tensor_tensor(hscr[:, 128:192].rearrange(
            "p (g c) -> p g c", c=8), hscr[:, 64:128].rearrange("p (g c) -> p g c", c=8),
            sB, ALU.mult))
        c_carry = ew("vector", lambda e: e.reduce_sum(smal[:, COL_CB:COL_CB + 8],
                                                      hscr[:, 128:192].rearrange(
                                                          "p (g c) -> p g c", c=8),
                                                      axis=AX.X), reduce=True)
        c_rwkv = 0
        for gch in range(G):
            ewb = smal[:, COL_EW + gch:COL_EW + gch + 1].broadcast_to([128, 2048])
            ew("vector", lambda e, g=gch, ewb=ewb: e.tensor_tensor_scan(
                scn[:, 0, :], ewb, bfD[:, g, 1:NCOL - 1],
                smal[:, COL_CA + g:COL_CA + g + 1], ALU.mult, ALU.add),
                deps=[(vsem, c_carry)], reduce=True)
            ew("vector", lambda e, g=gch, ewb=ewb: e.tensor_tensor_scan(
                scn[:, 1, :], ewb, bfC[:, g, 1:NCOL - 1],
                smal[:, COL_CB + g:COL_CB + g + 1], ALU.mult, ALU.add),
               reduce=True)
            c_scan2 = vsem.n
            ew("vector", lambda e, g=gch: e.scalar_tensor_tensor(
                hscr[:, 0:1], rsc3[:, 2, g:g + 1], smal[:, COL_EU + g:COL_EU + g + 1],
                smal[:, COL_CB + g:COL_CB + g + 1], ALU.mult, ALU.add), reduce=True)
            ew("vector", lambda e: e.reciprocal(hscr[:, 1:2], hscr[:, 0:1]), reduce=True)
            ew("vector", lambda e, g=gch: e.scalar_tensor_tensor(
                hscr[:, 2:3], rsc3[:, 1, g:g + 1], smal[:, COL_EU + g:COL_EU + g + 1],
                smal[:, COL_CA + g:COL_CA + g + 1], ALU.mult, ALU.add), reduce=True)
            ew("vector", lambda e: e.tensor_tensor(hscr[:, 2:3], hscr[:, 2:3],
                                                   hscr[:, 1:2], ALU.mult), reduce=True)
            for ti, (o, w_) in enumerate(TSPANS):
                so = o - 2
                wait(plan, "sync", vsem, c_rwkv)
                csr = dma("sync", sscr[:, 0, :], sr_hbm_re[:, gch, o - C0:o - C0 + w_])
                ew("vector", lambda e, g=gch, o=o, w_=w_, so=so: e.scalar_tensor_tensor(
                    hscr[:, o:o + w_], bfC[:, g, o:o + w_],
                    smal[:, COL_EU + g:COL_EU + g + 1],
                    scn[:, 1, so:so + w_], ALU.mult, ALU.add), deps=[(vsem, c_scan2)])
                ew("vector", lambda e, o=o, w_=w_: e.reciprocal(
                    bmu[:, o:o + w_], hscr[:, o:o + w_]), reduce=True)
                ew("vector", lambda e, g=gch, o=o, w_=w_, so=so: e.scalar_tensor_tensor(
                    brs[:, o:o + w_], bfD[:, g, o:o + w_],
                    smal[:, COL_EU + g:COL_EU + g + 1],
                    scn[:, 0, so:so + w_], ALU.mult, ALU.add))
                ew("vector", lambda e, o=o, w_=w_: e.tensor_tensor(
                    brs[:, o:o + w_], brs[:, o:o + w_], bmu[:, o:o + w_], ALU.mult))
                c_rwkv = ew("vector", lambda e, g=gch, o=o, w_=w_: e.tensor_tensor(
                    bfD[:, g, o:o + w_], brs[:, o:o + w_], sscr[:, 0, 0:w_], ALU.mult),
                    deps=[(ds, csr)])
            c_rwkv = ew("vector", lambda e, g=gch: e.tensor_tensor(
                bfD[:, g, 1:2], hscr[:, 2:3], rsc3[:, 3, g:g + 1], ALU.mult), reduce=True)
        if "tap_rwkv" in tap_aps:
            wait(plan, "gpsimd", vsem, c_rwkv)
            for gch in range(G):
                dma("gpsimd", tap_aps["tap_rwkv"].rearrange("(g p) t -> p g t", p=128)[:, gch, :],
                    bfD[:, gch, 0:HW])
        def x1_evac(oc, o, w_, bk, cm):
            c = ew("vector", lambda e, oc=oc, o=o, w_=w_, b=bk: e.scalar_tensor_tensor(
                bfA[:, oc, o:o + w_], b.ap[:, 0:w_], 1.0, bfA[:, oc, o:o + w_],
                ALU.mult, ALU.add), deps=[(tsem, cm)])
            bk.free_sem = (vsem, c)
            return c
        c_x1 = proj_pass(bfD, 1, x1_evac, c_wo, extra_deps=[(vsem, c_rwkv)])
        c_womm_done = tsem.n
        if "tap_x1" in tap_aps:
            wait(plan, "gpsimd", vsem, c_x1)
            for gch in range(G):
                dma("gpsimd", tap_aps["tap_x1"].rearrange("(g p) t -> p g t", p=128)[:, gch, :],
                    bfA[:, gch, 0:HW])

        stats3_ready, _ = ln_stats(bfA, bfC, (vsem, c_x1),
                                   extra_deps=[(tsem, c_womm_done)])
        c_xk2 = 0
        for gch in range(G):
            ew("vector", lambda e, g=gch: e.tensor_tensor(
                hscr[:, 0:NCOL], bfA[:, g, 0:NCOL], bmu[:, 0:NCOL], ALU.subtract),
                deps=[stats3_ready])
            ew("vector", lambda e: e.tensor_tensor(
                hscr[:, 0:NCOL], hscr[:, 0:NCOL], brs[:, 0:NCOL], ALU.mult))
            ew("vector", lambda e, g=gch: e.tensor_scalar(
                hscr[:, 0:NCOL], hscr[:, 0:NCOL], pcg("ln2_g", g), pcg("ln2_b", g),
                ALU.mult, ALU.add))
            ew("vector", lambda e: e.tensor_scalar_mul(
                hscr[:, 0:2], hscr[:, 0:2], smal[:, COL_VALID:COL_VALID + 1]),
               reduce=True)
            ew("vector", lambda e, g=gch: e.tensor_tensor(
                bfD[:, g, 1:NCOL], hscr[:, 1:NCOL], hscr[:, 0:NCOL - 1], ALU.subtract))
            c_xk2 = ew("vector", lambda e, g=gch: e.scalar_tensor_tensor(
                bfB[:, g, 1:NCOL], bfD[:, g, 1:NCOL], pcg("cmk", g),
                hscr[:, 0:NCOL - 1], ALU.mult, ALU.add))
        c_xr2 = 0
        for gch in range(G):
            c_xr2 = ew("vector", lambda e, g=gch: e.scalar_tensor_tensor(
                bfD[:, g, 1:NCOL], bfD[:, g, 1:NCOL], pcg("d_rk2", g),
                bfB[:, g, 1:NCOL], ALU.mult, ALU.add))
        c_wcr = loadW(wcr, 0, c_womm_done)
        sr2_hbm_re = sr2_hbm.rearrange("(g p) t -> p g t", p=128)
        c_sr2_dma = [None]
        for oc in range(G):
            sd = 0
            for (o, w_) in TSPANS:
                bk = get_bank()
                cm = mm_group(bk, [(wg(0, kc)[:, oc * 128:(oc + 1) * 128],
                                    bfD[:, kc, o:o + w_]) for kc in range(G)],
                              out_slice=bk.ap[:, 0:w_],
                              deps=[(dg, c_wcr), (vsem, c_xr2)])
                sd = ew("scalar", lambda e, o=o, w_=w_, b=bk: e.activation(
                    srg[:, o:o + w_], b.ap[:, 0:w_], AF.Sigmoid),
                    deps=[(tsem, cm)] + ([(ds, c_sr2_dma[0])] if (o == C0 and c_sr2_dma[0]) else []))
                bk.free_sem = (ssem, sd)
            wait(plan, "sync", ssem, sd)
            c_sr2_dma[0] = dma("sync", sr2_hbm_re[:, oc, :], srg[:, C0:C0 + N])
        c_crmm_done = tsem.n
        def wgC(kc):
            return bfC[:, 4 + kc // 2, (kc % 2) * 1024:(kc % 2) * 1024 + 1024]

        def kkslot(j):
            return bfC[:, j // 4, (j % 4) * 512:(j % 4) * 512 + 512]

        c_wckq = [None, None]
        c_kv_prev = c_crmm_done
        c_kvmm = {}
        kv_evac_cnt = {}
        c_wcvq = None
        for hq in range(4):
            par = hq % 2
            wait(plan, "gpsimd", tsem, c_kv_prev)
            c_wckq[par] = dma("gpsimd", wpool[:, par * 4:(par + 1) * 4, 0:2048],
                              wck[hq].rearrange("a p f -> p a f"))
            c_wcvq = dma("gpsimd", bfC[:, 4:8, 0:2048], wcv[hq].rearrange("a p f -> p a f"))
            for tq, (o, w_) in enumerate(TSPANS):
                for hc in range(G):
                    bk = get_bank()
                    cm = mm_group(bk, [(wg(par, kc)[:, hc * 128:(hc + 1) * 128],
                                        bfB[:, kc, o:o + w_]) for kc in range(G)],
                                  out_slice=bk.ap[:, 0:w_],
                                  deps=[(dg, c_wckq[par]), (vsem, c_xk2),
                                        (ssem, kv_evac_cnt.get((hq - 1, tq, 7), 0))])
                    cr_ = ew("vector", lambda e, b=bk, w_=w_: e.tensor_single_scalar(
                        sscr[:, 1, 0:w_], b.ap[:, 0:w_], 0.0, ALU.max),
                        deps=[(tsem, cm)])
                    bk.free_sem = (vsem, cr_)
                    cs_ = ew("scalar", lambda e, hc=hc, w_=w_: e.activation(
                        kkslot(hc)[:, 0:w_], sscr[:, 1, 0:w_], AF.Square),
                        deps=[(vsem, cr_)])
                    kv_evac_cnt[(hq, tq, hc)] = cs_
                for cc in range(G):
                    bk = get_bank()
                    cm = mm_group(bk, [(wgC(kc)[:, cc * 128:(cc + 1) * 128],
                                        kkslot(kc)[:, 0:w_]) for kc in range(G)],
                                  out_slice=bk.ap[:, 0:w_],
                                  deps=[(dg, c_wcvq), (ssem, kv_evac_cnt[(hq, tq, 7)])])
                    c_kvmm[(hq, tq, cc)] = cm
                    if hq == 0:
                        cv_ = ew("scalar", lambda e, cc=cc, o=o, w_=w_, b=bk: e.copy(
                            bfD[:, cc, o:o + w_], b.ap[:, 0:w_]), deps=[(tsem, cm)])
                        bk.free_sem = (ssem, cv_)
                    else:
                        cv_ = ew("vector", lambda e, cc=cc, o=o, w_=w_, b=bk: e.tensor_tensor(
                            bfD[:, cc, o:o + w_], bfD[:, cc, o:o + w_], b.ap[:, 0:w_],
                            ALU.add), deps=[(tsem, cm)])
                        bk.free_sem = (vsem, cv_)
            c_kv_prev = c_kvmm[(hq, 3, 7)]
        c_cmix_mm_done = c_kv_prev
        c_kv_v = vsem.n
        c_kv_s = ssem.n
        if "tap_kv" in tap_aps:
            wait(plan, "gpsimd", vsem, c_kv_v)
            wait(plan, "gpsimd", ssem, c_kv_s)
            for gch in range(G):
                dma("gpsimd", tap_aps["tap_kv"].rearrange("(g p) t -> p g t", p=128)[:, gch, :],
                    bfD[:, gch, 0:HW])
        g_hbm_re = g_hbm.rearrange("(g p) t -> p g t", p=128)
        c_comb = 0
        c_sr2r = None
        for cc in range(G):
            wait(plan, "sync", vsem, c_comb)
            cxr = dma("sync", hscr[:, 0:HW], x_re[:, cc, :])
            cgr = dma("sync", srg[:, 0:N], g_hbm_re[:, cc, :])
            for tq, (o, w_) in enumerate(TSPANS):
                wait(plan, "sync", vsem, c_comb)
                c_sr2r = dma("sync", sscr[:, 0, :], sr2_hbm_re[:, cc, o - C0:o - C0 + w_])
                ew("vector", lambda e, cc=cc, o=o, w_=w_: e.tensor_tensor(
                    sscr[:, 2, 0:w_], bfD[:, cc, o:o + w_], sscr[:, 0, 0:w_], ALU.mult),
                    deps=[(ds, max(c_sr2r, cxr, cgr)), (vsem, c_kv_v), (ssem, c_kv_s)])
                ew("vector", lambda e, cc=cc, o=o, w_=w_: e.tensor_tensor(
                    sscr[:, 2, 0:w_], sscr[:, 2, 0:w_], bfA[:, cc, o:o + w_], ALU.add))
                ew("vector", lambda e, cc=cc, o=o, w_=w_: e.tensor_tensor(
                    sscr[:, 2, 0:w_], sscr[:, 2, 0:w_], srg[:, o - C0:o - C0 + w_],
                    ALU.mult))
                c_comb = ew("vector", lambda e, cc=cc, o=o, w_=w_: e.tensor_tensor(
                    bfA[:, cc, o:o + w_], sscr[:, 2, 0:w_], hscr[:, o:o + w_], ALU.add))
        c_xnew = c_comb
        if "tap_xnew" in tap_aps:
            wait(plan, "gpsimd", vsem, c_xnew)
            for gch in range(G):
                dma("gpsimd", tap_aps["tap_xnew"].rearrange("(g p) t -> p g t", p=128)[:, gch, :],
                    bfA[:, gch, C0:C0 + N])

        stats4_ready, _ = ln_stats(bfA, bfC, (vsem, c_xnew))
        wait(plan, "sync", stats4_ready[0], stats4_ready[1])
        dma("sync", stat4_hbm[0:1, :], bmu[0:1, :])
        c_s4row = dma("sync", stat4_hbm[1:2, :], brs[0:1, :])
        c_z1 = 0
        for gch in range(G):
            ew("vector", lambda e, g=gch: e.tensor_tensor(
                hscr[:, C0:C0 + N], bfA[:, g, C0:C0 + N], bmu[:, C0:C0 + N],
                ALU.subtract), deps=[stats4_ready])
            c_z1 = ew("vector", lambda e, g=gch: e.tensor_tensor(
                bfB[:, g, C0:C0 + N], hscr[:, C0:C0 + N], brs[:, C0:C0 + N], ALU.mult))
        if "tap_z1" in tap_aps:
            wait(plan, "gpsimd", vsem, c_z1)
            for gch in range(G):
                dma("gpsimd", tap_aps["tap_z1"].rearrange("(g p) t -> p g t", p=128)[:, gch, :],
                    bfB[:, gch, C0:C0 + N])
        PIDX["qsks"] = len(PIDX)
        ew("vector", lambda e: e.tensor_tensor(pc("qsks"), pc("qs"), pc("ks"),
                                               ALU.mult), reduce=True)
        c_qsks = vsem.n
        ew("vector", lambda e: e.tensor_scalar_mul(hscr[:, 0:16], iota16sb[:],
                                                   smal[:, COL_LND:COL_LND + 1]),
           deps=[(dg, c_par)], reduce=True)
        cs_ = ew("scalar", lambda e: e.activation(rsc3[:, 0, :], hscr[:, 0:16], AF.Exp),
                 deps=[(vsem, vsem.n)])
        ew("vector", lambda e: e.tensor_scalar_mul(hscr[:, 16:17], smal[:, 89:90],
                                                   smal[:, COL_LND:COL_LND + 1]),
           reduce=True)
        ew("vector", lambda e: e.tensor_scalar(hscr[:, 17:18], smal[:, 89:90],
                                               -1.0, 127.0, ALU.mult, ALU.add),
           reduce=True)
        ew("vector", lambda e: e.tensor_scalar_mul(hscr[:, 18:19], hscr[:, 17:18],
                                                   smal[:, COL_LND:COL_LND + 1]),
           reduce=True)
        ew("scalar", lambda e: e.activation(hscr[:, 20:21], hscr[:, 16:17], AF.Exp),
           deps=[(vsem, vsem.n)])
        cs2_ = ew("scalar", lambda e: e.activation(hscr[:, 21:22], hscr[:, 18:19],
                                                   AF.Exp))
        ew("vector", lambda e: e.tensor_scalar_mul(rsc[:], rsc3[:, 0, :],
                                                   hscr[:, 20:21]),
           deps=[(ssem, cs2_)], reduce=True)
        c_rr = ew("vector", lambda e: e.tensor_scalar_mul(rsc3[:, 1, 0:4],
                                                          rsc3[:, 0, 0:4],
                                                          hscr[:, 21:22]), reduce=True)
        wait(plan, "gpsimd", vsem, c_rr)
        wait(plan, "gpsimd", gsem, gsem.n)
        wait(plan, "gpsimd", ssem, ssem.n)
        c_et = dma("gpsimd", hscr[:, 0:2048], et_in.rearrange("p a b -> p (a b)"))
        ew("vector", lambda e: e.tensor_single_scalar(bmu[:, 0:2048], hscr[:, 0:2048],
                                                      0.0, ALU.is_ge),
           deps=[(ds, c_s4row), (dg, c_et), (vsem, c_z1)])
        ew("vector", lambda e: e.tensor_scalar_mul(hscr[:, 0:2048], hscr[:, 0:2048],
                                                   smal[:, COL_LND:COL_LND + 1]))
        cvd = vsem.n
        csd = ew("scalar", lambda e: e.activation(sscr[:].rearrange("p a b -> p (a b)"),
                                                  hscr[:, 0:2048], AF.Exp),
                 deps=[(vsem, cvd)])
        c_dtm = ew("vector", lambda e: e.tensor_tensor(
            sscr[:].rearrange("p a b -> p (a b)"), sscr[:].rearrange("p a b -> p (a b)"),
            bmu[:, 0:2048], ALU.mult), deps=[(ssem, csd)])
        pTf = [0, 0]
        c_z1tok = 0
        for tg in range(16):
            for kc in range(G):
                i = tg * G + kc
                par = i % 2
                wait(plan, "tensor", vsem, max(c_z1, pTf[par]))
                ct = op(plan, "tensor", lambda e, tg=tg, kc=kc, par=par: e.transpose(
                    pT[par][:, 0:128], bfB[:, kc, C0 + tg * 128:C0 + (tg + 1) * 128],
                    ident[:]), inc=tsem)
                cc_ = ew("vector", lambda e, tg=tg, kc=kc, par=par: e.tensor_copy(
                    slot16(bfC, tg)[:, kc * 128:(kc + 1) * 128], pT[par][:, 0:128]),
                    deps=[(tsem, ct), (tsem, c_cmix_mm_done)])
                pTf[par] = cc_
                c_z1tok = cc_
        if "tap_ramp" in tap_aps:
            wait(plan, "gpsimd", vsem, max(c_rr, c_dtm))
            dma("gpsimd", tap_aps["tap_ramp"][0:128, 0:16], rsc[:])
            dma("gpsimd", tap_aps["tap_ramp"][0:128, 16:80],
                rsc3[:].rearrange("p a b -> p (a b)"))
            dma("gpsimd", tap_aps["tap_ramp"][0:128, 80:176], smal[:])
            dma("gpsimd", tap_aps["tap_ramp"][0:128, 176:2224],
                sscr[:].rearrange("p a b -> p (a b)"))
        if "tap_z1tok" in tap_aps:
            wait(plan, "gpsimd", vsem, c_z1tok)
            for gch in range(G):
                dma("gpsimd", tap_aps["tap_z1tok"].rearrange("(g p) t -> p g t", p=128)[:, gch, :],
                    bfC[:, gch, 0:HW])
        def kb(kc):
            return wpool[:, kc // 4, (kc % 4) * 512:(kc % 4) * 512 + 512]

        def ktb(sg):
            return wpool[:, 2 + sg // 2, (sg % 2) * 1024:(sg % 2) * 1024 + 1024]

        c_zev = {}
        c_kblk = 0
        c_ktb = 0
        for m in range(4):
            mo = C0 + m * 512
            for kc in range(G):
                c_kblk = ew("vector", lambda e, kc=kc, mo=mo: e.tensor_scalar_mul(
                    kb(kc), bfB[:, kc, mo:mo + 512], pcg("qsks", kc)),
                    deps=[(vsem, max(c_qsks, c_z1)), (tsem, c_cmix_mm_done),
                          (tsem, c_zev.get(m - 1, 0))])
            for sg in range(4):
                for kc in range(G):
                    i = sg * G + kc
                    par = i % 2
                    wait(plan, "tensor", vsem, max(c_kblk, pTf[par]))
                    ct = op(plan, "tensor", lambda e, sg=sg, kc=kc, par=par: e.transpose(
                        pT[par][:, 128:256], kb(kc)[:, sg * 128:(sg + 1) * 128],
                        ident[:]), inc=tsem)
                    cc_ = ew("vector", lambda e, sg=sg, kc=kc, par=par, m=m:
                             e.tensor_scalar_mul(ktb(sg)[:, kc * 128:(kc + 1) * 128],
                                                 pT[par][:, 128:256],
                                                 rsc[:, 4 * m + sg:4 * m + sg + 1]),
                             deps=[(tsem, ct), (vsem, c_rr)])
                    pTf[par] = cc_
                    c_ktb = cc_
            if m == 0 and "tap_ktb" in tap_aps:
                wait(plan, "gpsimd", vsem, c_ktb)
                dma("gpsimd", tap_aps["tap_ktb"][:, 0:2 * 2048].rearrange(
                    "p (a b) -> p a b", b=2048), wpool[:, 2:4, 0:2048])
                dma("gpsimd", tap_aps["tap_ktb"][:, 2 * 2048:4 * 2048].rearrange(
                    "p (a b) -> p a b", b=2048), wpool[:, 0:2, 0:2048])
            for cchunk in range(G):
                for sp2 in range(2):
                    bkz = get_bank()
                    cm = mm_group(bkz, [(ktb(sg)[:, cchunk * 128:(cchunk + 1) * 128],
                                         slot16(bfC, 4 * m + sg)[:, sp2 * 512:
                                                                 (sp2 + 1) * 512])
                                        for sg in range(4)],
                                  deps=[(vsem, max(c_ktb, c_z1tok))])
                    zsl = wg(1, cchunk)[:, sp2 * 512:(sp2 + 1) * 512]
                    if m == 0:
                        ce_ = ew("scalar", lambda e, b=bkz, zsl=zsl: e.copy(zsl, b.ap[:]),
                                 deps=[(tsem, cm)])
                        bkz.free_sem = (ssem, ce_)
                    else:
                        ce_ = ew("vector", lambda e, b=bkz, zsl=zsl: e.tensor_tensor(
                            zsl, zsl, b.ap[:], ALU.add), deps=[(tsem, cm)])
                        bkz.free_sem = (vsem, ce_)
                    c_zev[m] = cm
        c_z_v = vsem.n
        c_z_s = ssem.n
        if "tap_z" in tap_aps:
            wait(plan, "gpsimd", vsem, c_z_v)
            wait(plan, "gpsimd", ssem, c_z_s)
            dma("gpsimd", tap_aps["tap_z"][:, 0:4 * 2048].rearrange(
                "p (a b) -> p a b", b=2048), wpool[:, 4:8, 0:2048])
        wait(plan, "gpsimd", vsem, c_z_v)
        wait(plan, "gpsimd", ssem, c_z_s)
        c_zw = 0
        for kc in range(G):
            c_zw = dma("gpsimd", cc2_in.ap()[kc * 128:(kc + 1) * 128, :], wg(1, kc))
        wait(plan, "gpsimd", dg, c_zw)
        op(plan, "gpsimd", lambda e: e.collective_compute(
            "AllGather", ALU.bypass, replica_groups=[list(range(NCORE))],
            ins=[cc2_in.ap().opt()], outs=[cc2_out.ap().opt()]), inc=csem)
        c_cc2 = csem.n
        ew("vector", lambda e: e.tensor_scalar_mul(hscr[:, 0:8], smal[:, 17:25],
                                                   smal[:, COL_LND:COL_LND + 1]),
           reduce=True)
        cml = vsem.n
        if "tap_arg" in tap_aps:
            wait(plan, "gpsimd", vsem, cml)
            dma("gpsimd", tap_aps["tap_arg"][:, 0:8], hscr[:, 0:8])
            dma("gpsimd", tap_aps["tap_arg"][:, 8:16], smal[:, 17:25])
            dma("gpsimd", tap_aps["tap_arg"][:, 16:17], smal[:, COL_LND:COL_LND + 1],
                slow=True)
        cme = ew("scalar", lambda e: e.activation(hscr[:, 8:16], hscr[:, 0:8], AF.Exp,
                                                  scale=2048.0),
                 deps=[(vsem, cml)])
        c_l1w = ew("vector", lambda e: e.tensor_tensor(hscr[:, 8:16], hscr[:, 8:16],
                                                       smal[:, 25:33], ALU.mult),
                   deps=[(ssem, cme)], reduce=True)
        for kc in range(G):
            ew("vector", lambda e, kc=kc: e.memset(wg(0, kc), 0.0),
               deps=[(tsem, c_zev[3]), (vsem, c_z_v)])
        c_mz = vsem.n
        wait(plan, "gpsimd", csem, c_cc2)
        wait(plan, "gpsimd", vsem, c_mz)
        if "tap_cc2" in tap_aps:
            for blk in range(8):
                cdbg = dma("gpsimd", srg[:, 0:1024],
                           cc2_out.ap()[blk * 1024:blk * 1024 + 128, :])
                wait(plan, "gpsimd", dg, cdbg)
                cdd = dma("gpsimd", tap_aps["tap_cc2"][blk * 128:(blk + 1) * 128, :],
                          srg[:, 0:1024])
                wait(plan, "gpsimd", dg, cdd)
        c_minit = c_mz
        for cc in range(NCORE):
            for kc in range(G):
                czl = dma("gpsimd", srg[:, 0:1024],
                          cc2_out.ap()[cc * 1024 + kc * 128:cc * 1024 + (kc + 1) * 128, :])
                c_minit = ew("vector", lambda e, kc=kc, cc=cc: e.scalar_tensor_tensor(
                    wg(0, kc), srg[:, 0:1024], hscr[:, 8 + cc:9 + cc], wg(0, kc),
                    ALU.mult, ALU.add), deps=[(dg, czl), (vsem, c_l1w)])
                wait(plan, "gpsimd", vsem, c_minit)
        if "tap_l1w" in tap_aps:
            wait(plan, "gpsimd", vsem, c_minit)
            dma("gpsimd", tap_aps["tap_l1w"], hscr[:, 0:64])
        if "tap_minit" in tap_aps:
            wait(plan, "gpsimd", vsem, c_minit)
            dma("gpsimd", tap_aps["tap_minit"][:, 0:4 * 2048].rearrange(
                "p (a b) -> p a b", b=2048), wpool[:, 0:4, 0:2048])
        def scslot(i):
            return bfD[:, i // 4, (i % 4) * 512:(i % 4) * 512 + 512]

        c_mprime = 0
        c_sout = 0
        c_outmm = 0
        for j in range(4):
            jo = C0 + j * 512
            sc_idx = {}
            nsl = 0
            c_scmm = 0
            def kb2(kc):
                return bfD[:, 4 + kc // 4, (kc % 4) * 512:(kc % 4) * 512 + 512]
            for m in range(j, 4):
                mo = C0 + m * 512
                for kc in range(G):
                    c_kblk = ew("vector", lambda e, kc=kc, mo=mo: e.tensor_scalar_mul(
                        kb2(kc), bfB[:, kc, mo:mo + 512], pcg("qsks", kc)),
                        deps=[(vsem, c_minit if (j == 0 and m == j) else 0),
                              (tsem, max(c_scmm, c_outmm))])
                for ssub in range(4):
                    bks = get_bank()
                    cm = mm_group(bks, [(kb2(kc)[:, ssub * 128:(ssub + 1) * 128],
                                         bfB[:, kc, jo:jo + 512]) for kc in range(G)],
                                  deps=[(vsem, c_kblk)])
                    c_scmm = cm
                    sl = scslot(nsl)
                    sc_idx[(m, ssub)] = nsl
                    nsl += 1
                    if m == j:
                        ce_ = ew("vector", lambda e, b=bks, sl=sl, ssub=ssub:
                                 e.tensor_tensor(sl, b.ap[:], sscr[:, ssub, :],
                                                 ALU.mult),
                                 deps=[(tsem, cm), (vsem, c_dtm), (tsem, c_outmm)])
                    else:
                        ce_ = ew("vector", lambda e, b=bks, sl=sl, m=m, j=j, ssub=ssub:
                                 e.tensor_scalar_mul(sl, b.ap[:],
                                                     rsc[:, 4 * (m - j - 1) + ssub:
                                                         4 * (m - j - 1) + ssub + 1]),
                                 deps=[(tsem, cm), (tsem, c_outmm)])
                    bks.free_sem = (vsem, ce_)
            c_scb = vsem.n
            def mp(kc):
                return wg(1, kc)
            for kc in range(G):
                c_mprime = ew("vector", lambda e, kc=kc, j=j: e.tensor_scalar_mul(
                    mp(kc), wg(0, kc), rsc3[:, 0, 4 + (3 - j):5 + (3 - j)]
                    if False else rsc3[:, 0, (3 - j) * 4:(3 - j) * 4 + 1]),
                    deps=[(vsem, c_minit), (tsem, c_scmm)])
            for tsub in range(4):
                to = jo + tsub * 128
                for csp in range(2):
                    bk1 = get_bank()
                    pairs1 = [(scslot(sc_idx[(j, ssub)])[:, tsub * 128:(tsub + 1) * 128],
                               slot16(bfC, 4 * j + ssub)[:, csp * 512:(csp + 1) * 512])
                              for ssub in range(4)]
                    cm1 = mm_group(bk1, pairs1, deps=[(vsem, c_scb)])
                    bk2 = get_bank()
                    pairs2 = []
                    for m in range(j + 1, 4):
                        for ssub in range(4):
                            pairs2.append(
                                (scslot(sc_idx[(m, ssub)])[:, tsub * 128:(tsub + 1) * 128],
                                 slot16(bfC, 4 * m + ssub)[:, csp * 512:(csp + 1) * 512]))
                    for kc in range(G):
                        pairs2.append((bfB[:, kc, to:to + 128],
                                       mp(kc)[:, csp * 512:(csp + 1) * 512]))
                    cm2 = mm_group(bk2, pairs2, deps=[(vsem, max(c_scb, c_mprime))])
                    c_outmm = max(c_outmm, cm2, cm1)
                    ce1 = ew("vector", lambda e, b=bk2, tsub=tsub, csp=csp:
                             e.tensor_scalar_mul(srg[:, csp * 512:(csp + 1) * 512],
                                                 b.ap[:],
                                                 rsc3[:, 1, 3 - tsub:4 - tsub]),
                             deps=[(tsem, cm2)])
                    bk2.free_sem = (vsem, ce1)
                    ce2 = ew("vector", lambda e, b=bk1, csp=csp: e.tensor_tensor(
                        srg[:, csp * 512:(csp + 1) * 512],
                        srg[:, csp * 512:(csp + 1) * 512], b.ap[:], ALU.add),
                        deps=[(tsem, cm1)])
                    bk1.free_sem = (vsem, ce2)
                    c_sout = ew("vector", lambda e, csp=csp: e.tensor_scalar_mul(
                        srg[:, csp * 512:(csp + 1) * 512],
                        srg[:, csp * 512:(csp + 1) * 512], smal[:, COL_OSC:COL_OSC + 1]))
                wait(plan, "sync", vsem, c_sout)
                c_stw = dma("sync", s_tok_hbm[(j * 512 + tsub * 128):
                                              (j * 512 + (tsub + 1) * 128), :],
                            srg[:, 0:1024])
                wait(plan, "vector", ds, c_stw)
        c_l1attn_done = tsem.n

        c_w1 = loadW(w1, 0, c_l1attn_done)
        stats6_ready, _ = ln_stats(bfB, bfD, (vsem, c_z1),
                                   extra_deps=[(tsem, c_l1attn_done)])
        c_m2r2 = ew("vector", lambda e: e.tensor_tensor(
            bfB[:, 0, 0:NCOL], bmu[:, 0:NCOL], brs[:, 0:NCOL], ALU.mult),
            deps=[stats6_ready])
        wait(plan, "sync", stats6_ready[0], stats6_ready[1])
        c_r4 = dma("sync", hscr[0:1, 0:HW], stat4_hbm[0:1, :])
        c_r4b = dma("sync", hscr[32:33, 0:HW], stat4_hbm[1:2, :])
        cst1 = ew("scalar", lambda e: e.copy(bfD[0:1, 0, 0:NCOL], hscr[0:1, 0:NCOL]),
                  deps=[(ds, c_r4b)])
        cst2 = ew("scalar", lambda e: e.copy(bfD[0:1, 1, 0:NCOL], hscr[32:33, 0:NCOL]))
        for si, (o, w_) in enumerate(SPANS):
            bkm = get_bank()
            cmm = mm_group(bkm, [(ones_row[:], bfD[0:1, 0, o:o + w_])],
                           out_slice=bkm.ap[:, 0:w_],
                           deps=[(ssem, cst2), (vsem, c_m2r2)])
            cem = ew("scalar", lambda e, b=bkm, o=o, w_=w_: e.copy(
                bmu[:, o:o + w_], b.ap[:, 0:w_]), deps=[(tsem, cmm)])
            bkm.free_sem = (ssem, cem)
            bkr = get_bank()
            cmr = mm_group(bkr, [(ones_row[:], bfD[0:1, 1, o:o + w_])],
                           out_slice=bkr.ap[:, 0:w_], deps=[(ssem, cst2)])
            cer = ew("vector", lambda e, b=bkr, o=o, w_=w_: e.tensor_tensor(
                brs[:, o:o + w_], brs[:, o:o + w_], b.ap[:, 0:w_], ALU.mult),
                deps=[(tsem, cmr)])
            bkr.free_sem = (vsem, cer)
        c_fln = max(vsem.n, 1)
        c_flns = ssem.n

        def lnaff_f32(cc, dep_extra=()):
            ew("vector", lambda e, cc=cc: e.tensor_tensor(
                hscr[:, C0:C0 + N], bfA[:, cc, C0:C0 + N], bmu[:, C0:C0 + N],
                ALU.subtract),
                deps=[(vsem, c_fln), (ssem, c_flns)] + list(dep_extra))
            ew("vector", lambda e: e.tensor_tensor(
                hscr[:, C0:C0 + N], hscr[:, C0:C0 + N], brs[:, C0:C0 + N], ALU.mult))
            ew("vector", lambda e: e.tensor_tensor(
                hscr[:, C0:C0 + N], hscr[:, C0:C0 + N], bfB[:, 0, C0:C0 + N],
                ALU.subtract))
            return ew("vector", lambda e, cc=cc: e.tensor_scalar(
                hscr[:, C0:C0 + N], hscr[:, C0:C0 + N], pcg("gamma", cc),
                pcg("beta", cc), ALU.mult, ALU.add))

        for cc in range(G):
            c_lnaff = lnaff_f32(cc)
            ew("vector", lambda e, cc=cc: e.reduce_max(
                rsc[:, 8 + cc:9 + cc], hscr[:, C0:C0 + N], axis=AX.X,
                apply_absolute_value=True), reduce=True)
        c_lm = ew("vector", lambda e: e.reduce_max(rsc[:, 0:1], rsc[:, 8:16], axis=AX.X),
                  reduce=True)
        CEL = 1024
        ROWA, ROWB, ROWC, ONESR = 1040, 1168, 1296, 1424

        def cell(i):
            return hscr[0:1, CEL + i:CEL + i + 1]

        wait(plan, "gpsimd", vsem, c_lm)
        c_c3w = dma("gpsimd", cc3_in.ap(), rsc[:, 0:1])
        wait(plan, "gpsimd", dg, c_c3w)
        op(plan, "gpsimd", lambda e: e.collective_compute(
            "AllGather", ALU.bypass, replica_groups=[list(range(NCORE))],
            ins=[cc3_in.ap().opt()], outs=[cc3_out.ap().opt()]), inc=csem)
        c_cc3 = csem.n
        wait(plan, "sync", csem, c_cc3)
        c_c3r = dma("sync", hscr[0:1, 0:1024],
                    cc3_out.ap().rearrange("a b -> (a b)").rearrange(
                        "(o t) -> o t", o=1))
        ew("vector", lambda e: e.reduce_max(cell(0), hscr[0:1, 0:1024], axis=AX.X),
           deps=[(ds, c_c3r)], reduce=True)
        ew("vector", lambda e: e.reciprocal(cell(1), cell(0)), reduce=True)
        c_i127 = ew("gpsimd", lambda e: e.tensor_scalar_mul(cell(2), cell(1), 127.0),
                    deps=[(vsem, vsem.n)])
        c_ones = ew("vector", lambda e: e.memset(hscr[0:1, ONESR:ONESR + 128], 1.0),
                    deps=[(gsem, c_i127)], reduce=True)

        bc_prev = [0]

        def bcast(ci, col, deps=()):
            cv = ew("vector", lambda e, ci=ci: e.tensor_tensor(
                hscr[0:1, ROWA:ROWA + 128], hscr[0:1, ONESR:ONESR + 128],
                cell(ci).broadcast_to((1, 128)), ALU.mult),
                deps=[(vsem, c_ones), (ds, bc_prev[0])] + list(deps),
                reduce=True)
            wait(plan, "sync", vsem, cv)
            d1 = dma("sync", sred[0:1, :], hscr[0:1, ROWA:ROWA + 128])
            wait(plan, "sync", ds, d1)
            d2 = dma("sync", smal[:, col:col + 1], sred[0:1, :])
            bc_prev[0] = d2
            return d2

        c_bc_gmax = bcast(0, 90)
        c_bc_inv = bcast(2, 91, deps=[(gsem, c_i127)])
        c_aw = 0
        for kc in range(G):
            c_aw = ew("scalar", lambda e, kc=kc: e.activation(wg(1, kc), wg(0, kc),
                                                              AF.Abs),
                      deps=[(dg, c_w1), (tsem, c_outmm)])
        awflat = wpool[:, 4:8, 0:2048]
        ew("vector", lambda e: e.reduce_sum(rsc3[:, 2, 0:4], awflat, axis=AX.X),
           deps=[(ssem, c_aw)], reduce=True)
        ew("vector", lambda e: e.reduce_sum(rsc[:, 2:3], rsc3[:, 2, 0:4], axis=AX.X),
           reduce=True)
        c_sumw = vsem.n
        wait(plan, "sync", vsem, c_sumw)
        d_s1 = dma("sync", sred[1:2, :], rsc[:, 2:3])
        wait(plan, "sync", ds, d_s1)
        d_s2 = dma("sync", hscr[0:1, ROWB:ROWB + 128], sred[1:2, :])
        ew("vector", lambda e: e.reduce_sum(cell(3), hscr[0:1, ROWB:ROWB + 128],
                                            axis=AX.X), deps=[(ds, d_s2)],
           reduce=True)
        ew("vector", lambda e: e.tensor_scalar_mul(cell(4), cell(3),
                                                   0.7 / (1024.0 * 1024.0)),
           reduce=True)
        c_nd = ew("gpsimd", lambda e: e.tensor_scalar_mul(cell(5), cell(4), -1.0),
                  deps=[(vsem, vsem.n)])
        c_bc_del = bcast(4, 92)
        c_bc_ndel = bcast(5, 93, deps=[(gsem, c_nd)])
        bfDm1 = bfD[:, 0:4, 0:2048]
        bfDm2 = bfD[:, 4:8, 0:2048]
        ew("vector", lambda e: e.tensor_scalar(bfDm1, awflat, smal[:, 92:93], None,
                                               ALU.is_gt), deps=[(ds, c_bc_del)])
        ew("vector", lambda e: e.tensor_tensor(bfDm2, awflat, bfDm1, ALU.mult))
        ew("vector", lambda e: e.reduce_sum(rsc3[:, 2, 0:4], bfDm2, axis=AX.X),
           reduce=True)
        ew("vector", lambda e: e.reduce_sum(rsc[:, 3:4], rsc3[:, 2, 0:4], axis=AX.X),
           reduce=True)
        ew("vector", lambda e: e.reduce_sum(rsc3[:, 2, 0:4], bfDm1, axis=AX.X),
           reduce=True)
        c_s2 = ew("vector", lambda e: e.reduce_sum(rsc[:, 4:5], rsc3[:, 2, 0:4],
                                                   axis=AX.X), reduce=True)
        wait(plan, "sync", vsem, c_s2)
        wait(plan, "sync", ds, c_bc_ndel)
        d_a1 = dma("sync", sred[2:3, :], rsc[:, 3:4])
        d_a2 = dma("sync", sred[3:4, :], rsc[:, 4:5])
        wait(plan, "sync", ds, d_a2)
        d_a3 = dma("sync", hscr[0:1, ROWB:ROWB + 128], sred[2:3, :])
        d_a4 = dma("sync", hscr[0:1, ROWC:ROWC + 128], sred[3:4, :])
        ew("vector", lambda e: e.reduce_sum(cell(6), hscr[0:1, ROWB:ROWB + 128],
                                            axis=AX.X), deps=[(ds, d_a4)],
           reduce=True)
        ew("vector", lambda e: e.reduce_sum(cell(7), hscr[0:1, ROWC:ROWC + 128],
                                            axis=AX.X), reduce=True)
        cgm_ = ew("gpsimd", lambda e: e.tensor_scalar_max(cell(8), cell(7), 1.0),
                  deps=[(vsem, vsem.n)])
        crr_ = ew("vector", lambda e: e.reciprocal(cell(9), cell(8)),
                  deps=[(gsem, cgm_)], reduce=True)
        cal_ = ew("gpsimd", lambda e: e.tensor_tensor(cell(10), cell(6), cell(9),
                                                      ALU.mult),
                  deps=[(vsem, crr_)])
        csc1 = ew("vector", lambda e: e.tensor_scalar_mul(cell(11), cell(10),
                                                          1.0 / 127.0),
                  deps=[(gsem, cal_)], reduce=True)
        csc2 = ew("gpsimd", lambda e: e.tensor_tensor(cell(12), cell(11), cell(0),
                                                      ALU.mult),
                  deps=[(vsem, csc1)])
        c_bc_al = bcast(10, 94, deps=[(gsem, cal_)])
        c_bc_scal = bcast(12, 95, deps=[(gsem, csc2)])
        c_bcast_last = c_bc_scal
        c_wt = 0
        c_wtd = [0, 0]
        for kc in range(G):
            par = kc % 2
            hb = hscr[:, 0:1024] if par == 0 else hscr[:, 1028:2052]
            wait(plan, "sync", ds, c_bcast_last)
            wait(plan, "sync", vsem, c_wtd[par])
            cdw = dma("sync", hb,
                      w1[kc // 2, :, (kc % 2) * 1024:(kc % 2) * 1024 + 1024])
            ew("vector", lambda e, hb=hb: e.tensor_scalar(
                bfD[:, 0, 0:1024], hb, smal[:, 92:93], None, ALU.is_gt),
                deps=[(ds, cdw), (vsem, c_s2 if kc == 0 else 0)])
            ew("vector", lambda e, hb=hb: e.tensor_scalar(
                bfD[:, 0, 1024:2048], hb, smal[:, 93:94], None, ALU.is_lt))
            c_wt = ew("vector", lambda e, kc=kc: e.tensor_tensor(
                wg(1, kc), bfD[:, 0, 0:1024], bfD[:, 0, 1024:2048], ALU.subtract))
            c_wtd[par] = c_wt
        if "tap_ffs" in tap_aps:
            wait(plan, "gpsimd", ds, c_bcast_last)
            wait(plan, "gpsimd", vsem, vsem.n)
            wait(plan, "gpsimd", ssem, ssem.n)
            wait(plan, "gpsimd", gsem, gsem.n)
            dma("gpsimd", tap_aps["tap_ffs"], smal[:, 88:96])
        c_q = 0
        for cc in range(G):
            lnaff_f32(cc, dep_extra=[(ds, c_bcast_last)])
            ew("vector", lambda e: e.tensor_scalar(
                hscr[:, C0:C0 + N], hscr[:, C0:C0 + N], smal[:, 91:92], -127.0,
                ALU.mult, ALU.max))
            ew("vector", lambda e: e.tensor_scalar_min(hscr[:, C0:C0 + N],
                                                       hscr[:, C0:C0 + N], 127.0))
            ew("vector", lambda e, cc=cc: e.tensor_scalar(
                bfC[:, cc, C0:C0 + N], hscr[:, C0:C0 + N], C64, C64,
                ALU.add, ALU.subtract))
            ew("vector", lambda e: e.tensor_scalar(
                hscr[:, C0:C0 + N], hscr[:, C0:C0 + N], C32, C32,
                ALU.add, ALU.subtract))
            ew("vector", lambda e: e.tensor_scalar_mul(
                hscr[:, C0:C0 + N], hscr[:, C0:C0 + N],
                smal[:, COL_SC1:COL_SC1 + 1]))
            c_q = ew("vector", lambda e, cc=cc: e.scalar_tensor_tensor(
                bfC[:, cc, C0:C0 + N], bfC[:, cc, C0:C0 + N],
                smal[:, COL_SC2:COL_SC2 + 1], hscr[:, C0:C0 + N],
                ALU.mult, ALU.add))
        if "tap_q" in tap_aps:
            wait(plan, "gpsimd", vsem, c_q)
            for gch in range(G):
                dma("gpsimd",
                    tap_aps["tap_q"].rearrange("(g p) t -> p g t", p=128)[:, gch, :],
                    bfC[:, gch, 0:HW])
            for kc in range(G):
                dma("gpsimd",
                    tap_aps["tap_wt"].rearrange("(g p) t -> p g t", p=128)[:, kc, :],
                    wg(1, kc))
        c_g1 = 0
        srg_free = 0
        for oc in range(G):
            for ti, (o, w_) in enumerate(TSPANS):
                bk = get_bank()
                cm = mm_group(bk, [(wg(1, kc)[:, oc * 128:(oc + 1) * 128],
                                    bfC[:, kc, o:o + w_]) for kc in range(G)],
                              out_slice=bk.ap[:, 0:w_],
                              deps=[(vsem, max(c_wt, c_q))])
                cv_ = ew("vector", lambda e, b=bk, w_=w_, sl=ti: e.tensor_scalar_mul(
                    srg[:, sl * 512:sl * 512 + w_], b.ap[:, 0:w_],
                    smal[:, 95:96]),
                    deps=[(tsem, cm), (ds, c_bcast_last), (ssem, srg_free)])
                bk.free_sem = (vsem, cv_)
                c_g1 = ew("scalar", lambda e, oc=oc, o=o, w_=w_, sl=ti: e.activation(
                    bfD[:, oc, o:o + w_], srg[:, sl * 512:sl * 512 + w_], AF.Silu),
                    deps=[(vsem, cv_)])
                srg_free = c_g1
        if "tap_g1" in tap_aps:
            wait(plan, "gpsimd", ssem, c_g1)
            for gch in range(G):
                dma("gpsimd", tap_aps["tap_g1"].rearrange("(g p) t -> p g t", p=128)[:, gch, :],
                    bfD[:, gch, 0:HW])
        out_re = out.rearrange("(g p) t -> p g t", p=128)
        c_outd = None
        c_p7v = 0
        for cc in range(G):
            wait(plan, "sync", vsem, c_p7v)
            wait(plan, "sync", ssem, c_g1)
            c_st = None
            for tg in range(16):
                c_st = op(plan, "sync",
                          lambda e, tg=tg, cc=cc: e.dma_start_transpose(
                              srg[:, tg * 128:(tg + 1) * 128],
                              s_tok_hbm[tg * 128:(tg + 1) * 128,
                                        cc * 128:(cc + 1) * 128]),
                          inc=ds)
            if "tap_s1" in tap_aps:
                wait(plan, "gpsimd", ds, c_st)
                dma("gpsimd", tap_aps["tap_s1"].rearrange("(g p) t -> p g t", p=128)[:, cc, :],
                    srg[:, 0:N])
            for ti, (o, w_) in enumerate(TSPANS):
                ew("vector", lambda e, cc=cc, o=o, w_=w_: e.tensor_tensor(
                    sscr[:, 0, 0:w_], srg[:, o - C0:o - C0 + w_],
                    bfD[:, cc, o:o + w_], ALU.mult),
                    deps=[(ds, c_st), (ssem, c_g1)])
                c_p7v = ew("vector", lambda e, cc=cc, o=o, w_=w_: e.tensor_tensor(
                    hscr[:, o - C0:o - C0 + w_], sscr[:, 0, 0:w_],
                    bfA[:, cc, o:o + w_], ALU.add))
            wait(plan, "sync", vsem, c_p7v)
            c_outd = dma("sync", out_re[:, cc, :], hscr[:, 0:N])
            wait(plan, "vector", ds, c_outd)
        wait(plan, "sync", ds, c_outd)
        wait(plan, "gpsimd", ds, c_outd)

        with nc.Block() as block:
            def mk(name):
                def run(e):
                    for fn in plan.seq[name]:
                        fn(e)
                return run
            block.tensor(mk("tensor"))
            block.vector(mk("vector"))
            block.scalar(mk("scalar"))
            block.gpsimd(mk("gpsimd"))
            block.sync(mk("sync"))

    return nc


def _host_inputs(inputs, core):
    b, q = core // 4, core % 4
    t0 = q * N
    x = inputs["L0_attn_x"]
    xs = np.zeros((C, HW), np.float32)
    xs[:, C0:C0 + N] = np.ascontiguousarray(x[b, t0:t0 + N, :].T)
    if q > 0:
        xs[:, 0] = x[b, t0 - 2, :]
        xs[:, 1] = x[b, t0 - 1, :]
    TA = lambda a: np.ascontiguousarray(np.asarray(a, np.float32))
    Tt = lambda a: np.ascontiguousarray(np.asarray(a, np.float32).T)

    def wlay(wt):
        wt = np.asarray(wt, np.float32)
        f = wt.shape[1]
        return np.ascontiguousarray(
            wt.reshape(4, 2, 128, f).transpose(0, 2, 1, 3).reshape(4, 128, 2 * f))

    gu_cat = np.concatenate([inputs["L0_ffn_gate_w"], inputs["L0_ffn_up_w"]], axis=2)
    wckT = np.asarray(inputs["L0_attn_Wck"], np.float32).T
    wcvT = np.asarray(inputs["L0_attn_Wcv"], np.float32).T
    m = {
        "xT": xs,
        "wk": wlay(np.asarray(inputs["L0_attn_Wk"]).T),
        "wv": wlay(np.asarray(inputs["L0_attn_Wv"]).T),
        "wr": wlay(np.asarray(inputs["L0_attn_Wr"]).T),
        "wo": wlay(np.asarray(inputs["L0_attn_Wo"]).T),
        "wcr": wlay(np.asarray(inputs["L0_attn_Wcr"]).T),
        "wck": np.stack([wlay(wckT[:, i * 1024:(i + 1) * 1024]) for i in range(4)]),
        "wcv": np.stack([wlay(wcvT[i * 1024:(i + 1) * 1024, :]) for i in range(4)]),
        "router": TA(inputs["L0_ffn_router_w"]),
        "moe_gu": np.stack([wlay(gu_cat[e]) for e in range(16)]),
        "moe_dn": TA(inputs["L0_ffn_down_w"]),
        "sh_gu": np.ascontiguousarray(
            np.concatenate([inputs["L0_ffn_shared_gate_w"], inputs["L0_ffn_shared_up_w"]], axis=1)),
        "sh_dn": TA(inputs["L0_ffn_shared_down_w"]),
        "w1": wlay(np.asarray(inputs["L1_ffn_weight"]).T),
        "ln1_g": TA(inputs["L0_attn_ln1_g"]), "ln1_b": TA(inputs["L0_attn_ln1_b"]),
        "ln2_g": TA(inputs["L0_attn_ln2_g"]), "ln2_b": TA(inputs["L0_attn_ln2_b"]),
        "td": TA(inputs["L0_attn_time_decay"]), "tf": TA(inputs["L0_attn_time_first"]),
        "tmk": TA(inputs["L0_attn_tm_k"]).reshape(C),
        "tmv": TA(inputs["L0_attn_tm_v"]).reshape(C),
        "tmr": TA(inputs["L0_attn_tm_r"]).reshape(C),
        "cmk": TA(inputs["L0_attn_cm_k"]).reshape(C),
        "cmr": TA(inputs["L0_attn_cm_r"]).reshape(C),
        "qs": TA(inputs["L1_attn_q_scale"]), "ks": TA(inputs["L1_attn_k_scale"]),
        "gamma": TA(inputs["L1_ffn_gamma"]), "beta": TA(inputs["L1_ffn_beta"]),
        "dl": np.full((128, 1), float(np.asarray(inputs["L1_attn_decay_logit"])), np.float32),
        "osc": np.full((128, 1), float(np.asarray(inputs["L1_attn_out_scale"])), np.float32),
        "valid": np.full((128, 1), 1.0 if q > 0 else 0.0, np.float32),
        "ident_in": np.eye(128, dtype=np.float32),
        "iota128_in": np.arange(128, dtype=np.float32).reshape(128, 1),
        "iota16_in": np.tile((128.0 * np.arange(16, dtype=np.float32)).reshape(1, 16),
                             (128, 1)),
        "oh16_in": np.ascontiguousarray(
            np.broadcast_to(np.eye(16, dtype=np.float32)[:, :, None], (16, 16, 128))),
    }
    wpow = np.zeros(NCORE, np.float32); wmask = np.zeros(NCORE, np.float32)
    lpow = np.zeros(NCORE, np.float32); lmask = np.zeros(NCORE, np.float32)
    for cc in range(NCORE):
        bb, qq = cc // 4, cc % 4
        if bb == b and qq < q:
            wpow[cc] = float(q - 1 - qq); wmask[cc] = 1.0
        if bb == b and qq > q:
            lpow[cc] = float(qq - q - 1); lmask[cc] = 1.0
    pow64 = np.zeros((8, 8), np.float32); msk64 = np.zeros((8, 8), np.float32)
    for cc in range(NCORE):
        pow64[:, cc] = wpow[cc]; msk64[:, cc] = wmask[cc]
    m["wkv_pow"] = np.tile(pow64.reshape(1, 64), (128, 1))
    m["wkv_mask"] = np.tile(msk64.reshape(1, 64), (128, 1))
    m["l1_pow"] = np.tile(lpow, (128, 1))
    m["l1_mask"] = np.tile(lmask, (128, 1))
    sp = np.arange(512); tp = np.arange(512)
    diff = sp[:, None] - tp[None, :] - 1.0
    et = np.where(diff >= 0, diff, -1.0).astype(np.float32)
    m["et_in"] = np.ascontiguousarray(et.reshape(4, 128, 512).transpose(1, 0, 2))
    return m


def kernel(**inputs):
    nc = build_program()
    in_maps = [_host_inputs(inputs, c) for c in range(NCORE)]
    res = run_bass_kernel_spmd(nc, in_maps, core_ids=list(range(NCORE)))
    out = np.zeros((B, T, C), np.float32)
    for c in range(NCORE):
        b, q = c // 4, c % 4
        out[b, q * N:(q + 1) * N, :] = res.results[c]["out"].T
    return out
